# version 02
# baseline (speedup 1.0000x reference)
import numpy as np
import ml_dtypes
import concourse.bass as bass
import concourse.mybir as mybir
from concourse.bass_utils import run_bass_kernel_spmd
from contextlib import ExitStack

F32 = mybir.dt.float32
F16 = mybir.dt.float16
BF16 = mybir.dt.bfloat16
I32 = mybir.dt.int32
AF = mybir.ActivationFunctionType
OP = mybir.AluOpType
AX = mybir.AxisListType

P = 128
D = 2048
NDT = D // P
NH = 16
IO = 5632
NIT = IO // P
NPAIR = IO // P
SCH = 64
TC = 512
NTT = TC // P
GUW = D // 8
DNW = IO // 8
DQW = 512

ENGS = ("sync", "scalar", "vector", "gpsimd", "tensor")


class Tok:
    __slots__ = ("sem", "val")
    def __init__(self, sem, val):
        self.sem = sem
        self.val = val


class Prog:
    def __init__(self):
        self.ops = {e: [] for e in ENGS}
        self.count = {e: 0 for e in ENGS}
        self.buf = {}
        self.dma_sems = set()

    def op(self, eng, emit, reads=(), writes=(), extra=(), inc=1, **kwargs):
        reads = list(reads)
        writes = list(writes)
        for k in list(reads):
            if isinstance(k, tuple) and k and str(k[0]).startswith("pb"):
                reads.remove(k)
                writes.append(k)
        deps = list(extra)
        for k in reads:
            st = self.buf.get(k)
            if st and st[0] is not None:
                deps.append(st[0])
        for k in writes:
            st = self.buf.get(k)
            if st:
                if st[0] is not None:
                    deps.append(st[0])
                deps.extend(st[1])
        sem = kwargs.get("sem") or eng
        if sem != eng:
            self.dma_sems.add(sem)
        self.count[sem] = self.count.get(sem, 0) + inc
        tok = Tok(sem, self.count[sem])
        self.ops[eng].append((deps, emit, inc, tok))
        for k in reads:
            self.buf.setdefault(k, [None, []])[1].append(tok)
        for k in writes:
            self.buf[k] = [tok, []]
        return tok

    def dma(self, eng, emit, reads=(), writes=(), extra=(), sem=None):
        assert sem is not None
        return self.op(eng, emit, reads, writes, extra, inc=16, sem=sem)

    def emit(self, block, sems):
        for eng in ENGS:
            ops = self.ops[eng]
            if not ops:
                continue

            def make_section(eng, ops):
                def section(h):
                    seen = {}
                    for deps, emit, inc, tok in ops:
                        need = {}
                        for d in deps:
                            if d.val > need.get(d.sem, 0):
                                need[d.sem] = d.val
                        for dsem in sorted(need):
                            dval = need[dsem]
                            if dsem == tok.sem:
                                dval = min(dval, tok.val - inc)
                            if dval <= seen.get(dsem, 0):
                                continue
                            h.wait_ge(sems[dsem], dval)
                            seen[dsem] = dval
                        emit(h).then_inc(sems[tok.sem], inc)
                return section
            getattr(block, eng)(make_section(eng, ops))


def cdiv(a, b):
    return (a + b - 1) // b


C2, C1, CM1, EPSLN, CM05, EPSQ, I128, C0, EPSF, I2048 = range(10)
CONST_VALS = [2.0, 1.0, -1.0, 1e-5, -0.5, 1.1920928955078125e-07,
              1.0 / 128.0, 0.0, 1e-6, 1.0 / 2048.0]


def build_nc(S, stage="full"):
    import os as _os
    NHT_RUN = int(_os.environ.get("KN_NHT", "16"))
    WOUT_RUN = _os.environ.get("KN_WOUT", "1") == "1"
    NCH = S // TC
    nc = bass.Bass("TRN2", target_bir_lowering=False)

    x_in = nc.dram_tensor("x_in", [S, D], F32, kind="ExternalInput").ap()
    win_blk = nc.dram_tensor("win_blk", [NH, NDT, P, 5 * P], F32, kind="ExternalInput").ap()
    wout_blk = nc.dram_tensor("wout_blk", [NDT, P, D], F32, kind="ExternalInput").ap()
    gup_pk = nc.dram_tensor("gup_pk", [GUW, 2 * IO], I32, kind="ExternalInput").ap()
    gup_sc = nc.dram_tensor("gup_sc", [GUW, 2 * IO], F16, kind="ExternalInput").ap()
    dn_pk = nc.dram_tensor("dn_pk", [DNW, D], I32, kind="ExternalInput").ap()
    dn_sc = nc.dram_tensor("dn_sc", [DNW, D], F16, kind="ExternalInput").ap()
    idm_in = nc.dram_tensor("idm", [P, P], BF16, kind="ExternalInput").ap()
    cst_in = nc.dram_tensor("cst", [P, 12], F32, kind="ExternalInput").ap()
    out_hbm = nc.dram_tensor("out", [S, D], F32, kind="ExternalOutput").ap()

    wgu_bf = nc.dram_tensor("wgu_bf", [D, 2 * IO], BF16, kind="Internal").ap()
    wdn_bf = nc.dram_tensor("wdn_bf", [IO, D], BF16, kind="Internal").ap()
    wout_bf = nc.dram_tensor("wout_bf", [NDT, P, D], BF16, kind="Internal").ap()
    x1_hbm = nc.dram_tensor("x1_hbm", [S, D], F32, kind="Internal").ap()

    es = ExitStack()
    sb = lambda n, sh, dt: es.enter_context(nc.sbuf_tensor(n, sh, dt))
    ps = lambda n, sh, dt: es.enter_context(nc.psum_tensor(n, sh, dt))

    s_cst = sb("s_cst", [P, 12], F32)
    s_idm = sb("s_idm", [P, P], BF16)
    x_sb = [sb(f"x{t}", [P, D], F32) for t in range(NTT)]
    sq_scr = sb("sq_scr", [P, D], F32)
    xn_sb = [sb(f"xn{t}", [P, D], BF16) for t in range(NTT)]
    xnT = [sb(f"xnT{d}", [P, TC], BF16) for d in range(NDT)]
    win_buf = [sb(f"winb{i}", [P, 4 * 5 * P], BF16) for i in range(4)]
    NWOB = 4
    wsm_buf = [sb(f"wsmb{i}", [P, TC], BF16) for i in range(NWOB)]
    hy_sb = [sb(f"hy{i}", [P, TC], BF16) for i in range(NIT)]
    silu_sb = sb("silu_sb", [P, TC], BF16)
    a_qf = sb("a_qf", [P, TC], F32)
    a_vf = sb("a_vf", [P, TC], F32)
    a_kc = sb("a_kc", [P, TC], F32)
    a_as = sb("a_as", [P, TC], F32)
    a_gs = sb("a_gs", [P, TC], F32)
    a_u = sb("a_u", [P, TC], F32)
    a_a0 = sb("a_a0", [P, TC], F32)
    a_a1 = sb("a_a1", [P, TC], F32)
    a_pref = sb("a_pref", [P, TC], F32)
    a_q2 = sb("a_q2", [P, TC], BF16)
    s_mask = sb("s_mask", [P, TC], F32)
    s_row = sb("s_row", [1, TC], F32)
    s_row2 = sb("s_row2", [1, TC], F32)
    s_ones_c = sb("s_ones_c", [P, 1], BF16)
    s_ones_r = sb("s_ones_r", [1, P], F32)
    s_state = sb("s_state", [P, NH], F32)
    ln_mean = [sb(f"lnm{t}", [P, 1], F32) for t in range(NTT)]
    ln_ssum = [sb(f"lns{t}", [P, 1], F32) for t in range(NTT)]
    ln_m2 = [sb(f"ln2_{t}", [P, 1], F32) for t in range(NTT)]
    ln_var = [sb(f"lnv{t}", [P, 1], F32) for t in range(NTT)]
    ln_rstd = [sb(f"lnr{t}", [P, 1], F32) for t in range(NTT)]
    f_ssum = sb("f_ssum", [P, NTT * 4], F32)
    f_tot = [sb(f"ftot{t}", [P, 1], F32) for t in range(NTT)]
    dq_pk = sb("dq_pk", [P, DQW], I32)
    dq_sc = sb("dq_sc", [P, DQW], F16)
    dq_em = sb("dq_em", [P, DQW], I32)
    dq_sg = sb("dq_sg", [P, DQW], I32)
    dq_r1 = sb("dq_r1", [P, DQW], F32)
    dq_vp = sb("dq_vp", [P, DQW], F32)
    dq_vs = sb("dq_vs", [P, DQW], F32)
    dq_o = sb("dq_o", [P, DQW], BF16)
    wo_src = sb("wo_src", [P, D], F32)
    wo_bfs = sb("wo_bfs", [P, D], BF16)

    pb = [ps(f"pb{i}", [P, TC], F32) for i in range(5)]
    pb_misc = ps("pb_misc", [P, TC], F32)
    pb_tp = [ps(f"pbtp{i}", [P, 1024], BF16) for i in range(2)]

    DMA_SEMS = (["d_cst", "d_idm", "d_wos", "d_wobf", "d_dqpk", "d_dqsc",
                 "d_x1h", "d_out", "d_wdeqg", "d_wdeqd"] +
                [f"d_x{i}" for i in range(NTT)] +
                [f"d_win{i}" for i in range(4)] +
                [f"d_wsm{i}" for i in range(4)])
    sems = {e: es.enter_context(nc.semaphore(f"sem_{e}")) for e in ENGS}
    for snm in DMA_SEMS:
        sems[snm] = es.enter_context(nc.semaphore(snm))
    block = es.enter_context(nc.Block())

    pr = Prog()

    def cc(i, wp=P):
        return s_cst.ap()[0:wp, i:i + 1]

    with es:
        pr.dma("sync", lambda h: h.dma_start(out=s_cst[:], in_=cst_in), writes=[("cst",)], sem="d_cst")
        pr.dma("sync", lambda h: h.dma_start(out=s_idm[:], in_=idm_in), writes=[("idm",)], sem="d_idm")
        pr.op("vector", lambda v: v.memset(s_ones_c[:], 1.0), writes=[("onesc",)])
        pr.op("vector", lambda v: v.memset(s_ones_r[:], 1.0), writes=[("onesr",)])
        pr.op("vector", lambda v: v.memset(s_state[:], 0.0),
              writes=[("state", i) for i in range(NH)])
        pr.op("vector", lambda v: v.memset(s_mask[:], 1.0), writes=[("mask",)])
        pr.op("vector", lambda v: v.memset(s_mask.ap()[:, 0:TC:SCH], 0.0),
              reads=[("mask",)], writes=[("mask",)])
        for ct in range(NDT):
            pr.dma("sync", lambda h, ct=ct: h.dma_start(out=wo_src[:], in_=wout_blk[ct]),
                   writes=[("wosrc",)], sem="d_wos")
            pr.op("scalar", lambda s: s.activation(wo_bfs[:], wo_src[:], AF.Copy),
                  reads=[("wosrc",)], writes=[("wobfs",)])
            pr.dma("scalar", lambda h, ct=ct: h.dma_start(out=wout_bf[ct], in_=wo_bfs[:]),
                   reads=[("wobfs",)], writes=[("woutbf",)], sem="d_wobf")

        def dequant_unit(pk_hbm, sc_hbm, dst, w0, wp, c0, cw, j):
            kpk, ksc = ("dqpk",), ("dqsc",)
            wsem = "d_wdeqg" if dst is wgu_bf else "d_wdeqd"
            wkey = ("wdeq", wsem)
            if j == 0:
                pr.dma("sync", lambda h: h.dma_start(
                    out=dq_pk[0:wp, 0:cw], in_=pk_hbm[w0:w0 + wp, c0:c0 + cw]), writes=[kpk], sem="d_dqpk")
                pr.dma("sync", lambda h: h.dma_start(
                    out=dq_sc[0:wp, 0:cw], in_=sc_hbm[w0:w0 + wp, c0:c0 + cw]), writes=[ksc], sem="d_dqsc")
            sh = 22 - 4 * j
            if sh >= 0:
                pr.op("vector", lambda v: v.tensor_scalar(
                    dq_em[0:wp, 0:cw], dq_pk[0:wp, 0:cw], sh, 7 << 22,
                    OP.logical_shift_left, OP.bitwise_and), reads=[kpk], writes=[("dqem",)])
            else:
                pr.op("vector", lambda v: v.tensor_scalar(
                    dq_em[0:wp, 0:cw], dq_pk[0:wp, 0:cw], -sh, 7 << 22,
                    OP.logical_shift_right, OP.bitwise_and), reads=[kpk], writes=[("dqem",)])
            pr.op("vector", lambda v: v.tensor_scalar(
                dq_sg[0:wp, 0:cw], dq_pk[0:wp, 0:cw], 28 - 4 * j, -2147483648,
                OP.logical_shift_left, OP.bitwise_and), reads=[kpk], writes=[("dqsg",)])
            pr.op("vector", lambda v: v.tensor_scalar(
                dq_em[0:wp, 0:cw], dq_em[0:wp, 0:cw], 252 << 22, None, OP.add),
                reads=[("dqem",)], writes=[("dqem",)])
            pr.op("scalar", lambda s: s.activation(
                dq_r1[0:wp, 0:cw], dq_em.ap()[0:wp, 0:cw].bitcast(F32), AF.Relu,
                bias=cc(C1, wp), scale=cc(CM1, wp)),
                reads=[("dqem",), ("cst",)], writes=[("dqr1",)])
            pr.op("gpsimd", lambda g: g.tensor_tensor(
                dq_vp[0:wp, 0:cw], dq_em.ap()[0:wp, 0:cw].bitcast(F32), dq_r1[0:wp, 0:cw],
                OP.subtract), reads=[("dqem",), ("dqr1",)], writes=[("dqvp",)])
            pr.op("vector", lambda v: v.tensor_tensor(
                dq_vs.ap()[0:wp, 0:cw].bitcast(I32), dq_vp.ap()[0:wp, 0:cw].bitcast(I32),
                dq_sg[0:wp, 0:cw], OP.bitwise_or),
                reads=[("dqvp",), ("dqsg",)], writes=[("dqvs",)])
            pr.op("gpsimd", lambda g: g.tensor_tensor(
                dq_o[0:wp, 0:cw], dq_vs[0:wp, 0:cw], dq_sc[0:wp, 0:cw], OP.mult),
                reads=[("dqvs",), ksc], writes=[("dqo",)])
            pr.dma("scalar", lambda h: h.dma_start(
                out=dst.rearrange("(w e) n -> w e n", e=8)[w0:w0 + wp, j, c0:c0 + cw],
                in_=dq_o[0:wp, 0:cw]), reads=[("dqo",)], writes=[wkey], sem=wsem)

        dq_units = []
        for wt in range(cdiv(GUW, P)):
            w0 = wt * P
            wp = min(P, GUW - w0)
            for ci in range(cdiv(2 * IO, DQW)):
                c0 = ci * DQW
                cw = min(DQW, 2 * IO - c0)
                for j in range(8):
                    dq_units.append((gup_pk, gup_sc, wgu_bf, w0, wp, c0, cw, j))
        for wt in range(cdiv(DNW, P)):
            w0 = wt * P
            wp = min(P, DNW - w0)
            for ci in range(cdiv(D, DQW)):
                c0 = ci * DQW
                cw = min(DQW, D - c0)
                for j in range(8):
                    dq_units.append((dn_pk, dn_sc, wdn_bf, w0, wp, c0, cw, j))
        n_gup_units = cdiv(GUW, P) * cdiv(2 * IO, DQW) * 8
        dq_pos = [0]

        def emit_dq(n):
            stop = min(len(dq_units), dq_pos[0] + n)
            while dq_pos[0] < stop:
                dequant_unit(*dq_units[dq_pos[0]])
                dq_pos[0] += 1

        def emit_dq_upto(limit):
            while dq_pos[0] < limit:
                dequant_unit(*dq_units[dq_pos[0]])
                dq_pos[0] += 1

        def ln_pass(src):
            for tt in range(NTT):
                kx = ("x", tt)
                pr.op("vector", lambda v, tt=tt: v.reduce_sum(
                    ln_mean[tt][:], src[tt][:], axis=AX.X), reads=[kx], writes=[("lnm", tt)])
                pr.op("vector", lambda v, tt=tt: v.tensor_scalar(
                    ln_mean[tt][:], ln_mean[tt][:], 1.0 / D, None, OP.mult),
                    reads=[("lnm", tt)], writes=[("lnm", tt)])
                pr.op("scalar", lambda s, tt=tt: s.activation(
                    sq_scr[:], src[tt][:], AF.Square, bias=cc(C0), scale=cc(C1),
                    accum_out=ln_ssum[tt][:]),
                    reads=[kx, ("cst",)], writes=[("sqscr",), ("lns", tt)])
                pr.op("scalar", lambda s, tt=tt: s.activation(
                    ln_m2[tt][:], ln_mean[tt][:], AF.Square, bias=cc(C0), scale=cc(C1)),
                    reads=[("lnm", tt), ("cst",)], writes=[("ln2", tt)])
                pr.op("vector", lambda v, tt=tt: v.scalar_tensor_tensor(
                    ln_var[tt][:], ln_ssum[tt][:], 1.0 / D, ln_m2[tt][:], OP.mult, OP.subtract),
                    reads=[("lns", tt), ("ln2", tt)], writes=[("lnv", tt)])
                pr.op("scalar", lambda s, tt=tt: s.activation(
                    ln_var[tt][:], ln_var[tt][:], AF.Ln, bias=cc(EPSLN), scale=cc(C1)),
                    reads=[("lnv", tt), ("cst",)], writes=[("lnv", tt)])
                pr.op("scalar", lambda s, tt=tt: s.activation(
                    ln_rstd[tt][:], ln_var[tt][:], AF.Exp, bias=cc(C0), scale=cc(CM05)),
                    reads=[("lnv", tt), ("cst",)], writes=[("lnr", tt)])
                pr.op("vector", lambda v, tt=tt: v.tensor_scalar(
                    xn_sb[tt][:], src[tt][:], ln_mean[tt][:], ln_rstd[tt][:],
                    OP.subtract, OP.mult),
                    reads=[kx, ("lnm", tt), ("lnr", tt)], writes=[("xn", tt)])

        def transpose_pass():
            for tt in range(NTT):
                for dt in range(NDT):
                    par = (tt * NDT + dt) % 2
                    ktp = ("pbtp", par)
                    def tp_emit(t, tt=tt, dt=dt, par=par):
                        t.transpose(pb_tp[par].ap()[:, 0:P],
                                    xn_sb[tt].ap()[:, dt * P:(dt + 1) * P], s_idm[:])
                        return t.drain(fusable=False)
                    pr.op("tensor", tp_emit, reads=[("xn", tt), ("idm",)], writes=[ktp])
                    pr.op("scalar", lambda s, tt=tt, dt=dt, par=par: s.activation(
                        xnT[dt].ap()[:, tt * P:(tt + 1) * P], pb_tp[par].ap()[:, 0:P], AF.Copy),
                        reads=[ktp], writes=[("xnT", dt)])

        for ch in range(NCH):
            base = ch * TC
            for tt in range(NTT):
                pr.dma("sync", lambda h, tt=tt, base=base: h.dma_start(
                    out=x_sb[tt][:], in_=x_in[base + tt * P: base + (tt + 1) * P, :]),
                    writes=[("x", tt)],
                    extra=[Tok("d_x1h", pr.count.get("d_x1h", 0)),
                           Tok("d_out", pr.count.get("d_out", 0))], sem=f"d_x{tt}")
            ln_pass(x_sb)
            transpose_pass()
            if stage == "ln1":
                for tt in range(NTT):
                    pr.op("vector", lambda v, tt=tt: v.tensor_copy(x_sb[tt][:], xn_sb[tt][:]),
                          reads=[("xn", tt)], writes=[("x", tt)])
                    pr.dma("sync", lambda h, tt=tt, base=base: h.dma_start(
                        out=out_hbm[base + tt * P: base + (tt + 1) * P, :], in_=x_sb[tt][:]),
                        reads=[("x", tt)], writes=[("outh", ch, tt)], sem="d_out")
                continue

            for ht in range(NHT_RUN):
                for q in range(4):
                    pr.dma("gpsimd", lambda g, ht=ht, q=q: g.dma_start(
                        out=win_buf[q].ap().rearrange("p (a c) -> p a c", a=4),
                        in_=win_blk[ht, q * 4:(q + 1) * 4].transpose([1, 0, 2])),
                        writes=[("winb", q)], sem=f"d_win{q}")

                def proj_q(projs, banks, q, ht=ht):
                    def g_emit(t, projs=projs, banks=banks, q=q, ht=ht):
                        for dto in range(4):
                            dt = q * 4 + dto
                            coff = dto * 5 * P
                            for pj, bk in zip(projs, banks):
                                t.matmul(pb[bk][:],
                                         win_buf[q].ap()[:, coff + pj * P: coff + (pj + 1) * P],
                                         xnT[dt][:],
                                         start=(dt == 0), stop=(dt == NDT - 1))
                        return t.drain(fusable=False)
                    reads = [("winb", q)] + [("xnT", q * 4 + i) for i in range(4)]
                    extra_rw = [("pb", b) for b in banks]
                    if q == 0:
                        return pr.op("tensor", g_emit, reads=reads, writes=extra_rw)
                    return pr.op("tensor", g_emit, reads=reads + extra_rw, writes=extra_rw)

                for q in range(4):
                    proj_q([1, 2, 3], [0, 1, 2], q)
                for q in range(4):
                    proj_q([0, 4], [3, 4], q)
                ABL = _os.environ.get("KN_ABL", "all")
                if ABL == "proj":
                    continue

                NEV = int(_os.environ.get("KN_EV", "7"))
                if NEV >= 1:
                    pr.op("scalar", lambda s: s.activation(
                        a_kc[:], pb[0][:], AF.Tanh, bias=cc(C0), scale=cc(C1)),
                        reads=[("pb", 0), ("cst",)], writes=[("akc",)])
                if NEV >= 2:
                    pr.op("vector", lambda v: v.tensor_copy(a_vf[:], pb[1][:]),
                          reads=[("pb", 1)], writes=[("avf",)])
                if NEV >= 3:
                    pr.op("scalar", lambda s: s.activation(
                        a_as[:], pb[2][:], AF.Sigmoid, bias=cc(C2), scale=cc(C1)),
                        reads=[("pb", 2), ("cst",)], writes=[("aas",)])
                if NEV >= 4:
                    pr.op("vector", lambda v: v.tensor_scalar(
                        a_as[:], a_as[:], 0.6, 0.9995, OP.max, OP.min),
                        reads=[("aas",)], writes=[("aas",)])
                if NEV >= 5:
                    pr.op("scalar", lambda s: s.activation(
                        a_q2[:], pb[3][:], AF.Square, bias=cc(C0), scale=cc(C1)),
                        reads=[("pb", 3), ("cst",)], writes=[("aq2",)])
                if NEV >= 6:
                    pr.op("vector", lambda v: v.tensor_copy(a_qf[:], pb[3][:]),
                          reads=[("pb", 3)], writes=[("aqf",)])
                if NEV >= 7:
                    pr.op("scalar", lambda s: s.activation(
                        a_gs[:], pb[4][:], AF.Sigmoid, bias=cc(C0), scale=cc(C1)),
                        reads=[("pb", 4), ("cst",)], writes=[("ags",)])
                if ABL == "evac":
                    continue
                def rs_emit(t):
                    t.matmul(pb_misc.ap()[0:1, :], s_ones_c[:], a_q2[:], start=True, stop=True)
                    return t.drain(fusable=False)
                pr.op("tensor", rs_emit, reads=[("aq2",), ("onesc",)], writes=[("pbm",)])
                pr.op("scalar", lambda s: s.activation(
                    s_row[:], pb_misc.ap()[0:1, :], AF.Ln, bias=cc(EPSQ, 1), scale=cc(I128, 1)),
                    reads=[("pbm",), ("cst",)], writes=[("rowa",)])
                pr.op("scalar", lambda s: s.activation(
                    s_row2[:], s_row[:], AF.Exp, bias=cc(C0, 1), scale=cc(CM05, 1)),
                    reads=[("rowa",), ("cst",)], writes=[("rowb",)])
                def bc_emit(t):
                    t.matmul(pb_misc[:], s_ones_r[:], s_row2[:], start=True, stop=True)
                    return t.drain(fusable=False)
                pr.op("tensor", bc_emit, reads=[("rowb",), ("onesr",)], writes=[("pbm",)])
                pr.op("vector", lambda v: v.tensor_tensor(
                    a_qf[:], a_qf[:], pb_misc[:], OP.mult),
                    reads=[("aqf",), ("pbm",)], writes=[("aqf",)])
                if ABL == "qrms":
                    continue
                pr.op("vector", lambda v: v.tensor_tensor(a_u[:], a_kc[:], a_vf[:], OP.mult),
                      reads=[("akc",), ("avf",)], writes=[("au",)])
                pr.op("vector", lambda v: v.tensor_tensor(a_a0[:], a_as[:], s_mask[:], OP.mult),
                      reads=[("aas",), ("mask",)], writes=[("aa0",)])
                pr.op("vector", lambda v: v.tensor_tensor(a_a1[:], a_as[:], a_a0[:], OP.subtract),
                      reads=[("aas",), ("aa0",)], writes=[("aa1",)])
                pr.op("vector", lambda v: v.tensor_tensor_scan(
                    a_pref[:], a_a0[:], a_a1[:], 1.0, OP.mult, OP.add),
                    reads=[("aa0",), ("aa1",)], writes=[("apref",)])
                pr.op("vector", lambda v: v.tensor_scalar(
                    a_pref[:], a_pref[:], 1e6, 1.0, OP.mult, OP.min),
                    reads=[("apref",)], writes=[("apref",)])
                pr.op("vector", lambda v: v.tensor_tensor(
                    a_pref[:], a_u[:], a_pref[:], OP.mult),
                    reads=[("au",), ("apref",)], writes=[("apref",)])
                if ch == 0:
                    pr.op("vector", lambda v: v.tensor_tensor_scan(
                        a_a0[:], a_as[:], a_pref[:], 0.0, OP.mult, OP.add),
                        reads=[("aas",), ("apref",)], writes=[("aa0",)])
                else:
                    pr.op("vector", lambda v, ht=ht: v.tensor_tensor_scan(
                        a_a0[:], a_as[:], a_pref[:], s_state.ap()[:, ht:ht + 1],
                        OP.mult, OP.add),
                        reads=[("aas",), ("apref",), ("state", ht)], writes=[("aa0",)])
                if ch < NCH - 1:
                    pr.op("vector", lambda v, ht=ht: v.tensor_copy(
                        s_state.ap()[:, ht:ht + 1], a_a0.ap()[:, TC - 1:TC]),
                        reads=[("aa0",)], writes=[("state", ht)])
                if ABL == "scan":
                    continue
                pr.op("vector", lambda v: v.tensor_tensor(a_kc[:], a_qf[:], a_a0[:], OP.mult),
                      reads=[("aqf",), ("aa0",)], writes=[("akc",)])
                pr.op("vector", lambda v: v.tensor_tensor(a_kc[:], a_kc[:], a_vf[:], OP.subtract),
                      reads=[("akc",), ("avf",)], writes=[("akc",)])
                pr.op("vector", lambda v: v.tensor_tensor(a_kc[:], a_kc[:], a_gs[:], OP.mult),
                      reads=[("akc",), ("ags",)], writes=[("akc",)])
                pr.op("vector", lambda v, ht=ht: v.tensor_tensor(
                    hy_sb[ht][:], a_kc[:], a_vf[:], OP.add),
                    reads=[("akc",), ("avf",)], writes=[("hy", ht)])
                if stage == "full":
                    emit_dq(max(1, n_gup_units // (4 * NH)))

            for oc in range(4 if WOUT_RUN else 0):
                for ct in range(NDT):
                    bi = ct % NWOB
                    pr.dma("sync", lambda h, oc=oc, ct=ct, bi=bi: h.dma_start(
                        out=wsm_buf[bi][:], in_=wout_bf[ct][:, oc * TC:(oc + 1) * TC]),
                        reads=[("woutbf",)], writes=[("wsm", bi)], sem=f"d_wsm{bi}")
                    def wo_emit(t, oc=oc, ct=ct, bi=bi):
                        for tt in range(NTT):
                            t.matmul(pb[tt][:],
                                     hy_sb[ct].ap()[:, tt * P:(tt + 1) * P],
                                     wsm_buf[bi][:],
                                     start=(ct == 0), stop=(ct == NDT - 1))
                        return t.drain(fusable=False)
                    wr = [("pb", tt) for tt in range(NTT)]
                    pr.op("tensor", wo_emit,
                          reads=[("hy", ct), ("wsm", bi)] + (wr if ct > 0 else []),
                          writes=wr)
                for tt in range(NTT):
                    pr.op("vector", lambda v, oc=oc, tt=tt: v.tensor_tensor(
                        x_sb[tt].ap()[:, oc * TC:(oc + 1) * TC], pb[tt][:],
                        x_sb[tt].ap()[:, oc * TC:(oc + 1) * TC], OP.add),
                        reads=[("pb", tt), ("x", tt)], writes=[("x", tt)])
            for tt in range(NTT):
                pr.dma("sync", lambda h, tt=tt, base=base: h.dma_start(
                    out=x1_hbm[base + tt * P: base + (tt + 1) * P, :], in_=x_sb[tt][:]),
                    reads=[("x", tt)], writes=[("x1h", ch, tt)], sem="d_x1h")
            if stage == "attn":
                for tt in range(NTT):
                    pr.dma("sync", lambda h, tt=tt, base=base: h.dma_start(
                        out=out_hbm[base + tt * P: base + (tt + 1) * P, :], in_=x_sb[tt][:]),
                        reads=[("x", tt)], writes=[("outh", ch, tt)])

        if stage == "full":
            emit_dq_upto(n_gup_units)

        for ch in range(NCH if stage == "full" else 0):
            base = ch * TC
            for tt in range(NTT):
                pr.dma("sync", lambda h, tt=tt, base=base: h.dma_start(
                    out=x_sb[tt][:], in_=x1_hbm[base + tt * P: base + (tt + 1) * P, :]),
                    reads=[("x1h", ch, tt)], writes=[("x", tt)],
                    extra=[Tok("d_x1h", pr.count.get("d_x1h", 0)),
                           Tok("d_out", pr.count.get("d_out", 0))], sem=f"d_x{tt}")
            ln_pass(x_sb)
            transpose_pass()

            wgu_r = wgu_bf.rearrange("(dt p) n -> dt p n", p=P)
            for pair in range(NPAIR):
                pbg = pb[0] if pair % 2 == 0 else pb[2]
                pbu = pb[1] if pair % 2 == 0 else pb[3]
                kg = ("pb", 0 if pair % 2 == 0 else 2)
                ku = ("pb", 1 if pair % 2 == 0 else 3)
                for q in range(4):
                    bi = q
                    def gu_load_g(h, pair=pair, q=q, bi=bi):
                        src = wgu_r[q * 4:(q + 1) * 4, :, pair * P:(pair + 1) * P]
                        return h.dma_start(
                            out=win_buf[bi].ap()[:, 0:4 * P].rearrange("p (a c) -> p a c", a=4),
                            in_=src.transpose([1, 0, 2]))
                    def gu_load_u(h, pair=pair, q=q, bi=bi):
                        src = wgu_r[q * 4:(q + 1) * 4, :, IO + pair * P: IO + (pair + 1) * P]
                        return h.dma_start(
                            out=win_buf[bi].ap()[:, 4 * P:8 * P].rearrange("p (a c) -> p a c", a=4),
                            in_=src.transpose([1, 0, 2]))
                    pr.dma("sync", gu_load_g, reads=[("wdeq", "d_wdeqg")], writes=[("winb", bi)], sem=f"d_win{bi}")
                    pr.dma("sync", gu_load_u, reads=[("wdeq", "d_wdeqg")], writes=[("winb", bi)], sem=f"d_win{bi}")
                    def gu_mm(t, pair=pair, q=q, bi=bi, pbg=pbg, pbu=pbu):
                        for dto in range(4):
                            dt = q * 4 + dto
                            t.matmul(pbg[:], win_buf[bi].ap()[:, dto * P:(dto + 1) * P],
                                     xnT[dt][:], start=(dt == 0), stop=(dt == NDT - 1))
                            t.matmul(pbu[:], win_buf[bi].ap()[:, (4 + dto) * P:(5 + dto) * P],
                                     xnT[dt][:], start=(dt == 0), stop=(dt == NDT - 1))
                        return t.drain(fusable=False)
                    rd = [("winb", bi)] + [("xnT", q * 4 + i) for i in range(4)]
                    pr.op("tensor", gu_mm,
                          reads=rd + ([kg, ku] if q > 0 else []),
                          writes=[kg, ku])
                pr.op("scalar", lambda s, pbg=pbg: s.activation(
                    silu_sb[:], pbg[:], AF.Silu, bias=cc(C0), scale=cc(C1)),
                    reads=[kg, ("cst",)], writes=[("silu",)])
                pr.op("vector", lambda v, pair=pair, pbu=pbu: v.tensor_tensor(
                    hy_sb[pair][:], silu_sb[:], pbu[:], OP.mult),
                    reads=[("silu",), ku], writes=[("hy", pair)])
                emit_dq(max(1, (len(dq_units) - n_gup_units) // (2 * NPAIR)))
            emit_dq_upto(len(dq_units))

            wdn_r = wdn_bf.rearrange("(it p) n -> it p n", p=P)
            for hc in range(4):
                for it in range(NIT):
                    bi = it % NWOB
                    pr.dma("sync", lambda h, hc=hc, it=it, bi=bi: h.dma_start(
                        out=wsm_buf[bi][:], in_=wdn_r[it, :, hc * TC:(hc + 1) * TC]),
                        reads=[("wdeq", "d_wdeqd")], writes=[("wsm", bi)], sem=f"d_wsm{bi}")
                    def dn_mm(t, hc=hc, it=it, bi=bi):
                        for tt in range(NTT):
                            t.matmul(pb[tt][:],
                                     hy_sb[it].ap()[:, tt * P:(tt + 1) * P],
                                     wsm_buf[bi][:],
                                     start=(it == 0), stop=(it == NIT - 1))
                        return t.drain(fusable=False)
                    wr = [("pb", tt) for tt in range(NTT)]
                    pr.op("tensor", dn_mm,
                          reads=[("hy", it), ("wsm", bi)] + (wr if it > 0 else []),
                          writes=wr)
                for tt in range(NTT):
                    t_cp = pr.op("vector", lambda v, hc=hc, tt=tt: v.tensor_copy(
                        xn_sb[tt].ap()[:, hc * TC:(hc + 1) * TC], pb[tt][:]),
                        reads=[("pb", tt)], writes=[("xn", tt)])
                    pr.op("scalar", lambda s, hc=hc, tt=tt: s.activation(
                        sq_scr.ap()[:, 0:TC], pb[tt][:], AF.Square, bias=cc(C0), scale=cc(C1),
                        accum_out=f_ssum.ap()[:, tt * 4 + hc: tt * 4 + hc + 1]),
                        reads=[("pb", tt), ("cst",)], extra=[t_cp],
                        writes=[("sqscr",), ("fss", tt, hc)])
            for tt in range(NTT):
                pr.op("vector", lambda v, tt=tt: v.reduce_sum(
                    f_tot[tt][:], f_ssum.ap()[:, tt * 4:(tt + 1) * 4], axis=AX.X),
                    reads=[("fss", tt, hc2) for hc2 in range(4)], writes=[("ftot", tt)])
                pr.op("scalar", lambda s, tt=tt: s.activation(
                    f_tot[tt][:], f_tot[tt][:], AF.Ln, bias=cc(EPSF), scale=cc(I2048)),
                    reads=[("ftot", tt), ("cst",)], writes=[("ftot", tt)])
                pr.op("scalar", lambda s, tt=tt: s.activation(
                    f_tot[tt][:], f_tot[tt][:], AF.Exp, bias=cc(C0), scale=cc(CM05)),
                    reads=[("ftot", tt), ("cst",)], writes=[("ftot", tt)])
                pr.op("vector", lambda v, tt=tt: v.scalar_tensor_tensor(
                    x_sb[tt][:], xn_sb[tt][:], f_tot[tt][:], x_sb[tt][:], OP.mult, OP.add),
                    reads=[("xn", tt), ("ftot", tt), ("x", tt)], writes=[("x", tt)])
                pr.dma("sync", lambda h, tt=tt, base=base: h.dma_start(
                    out=out_hbm[base + tt * P: base + (tt + 1) * P, :], in_=x_sb[tt][:]),
                    reads=[("x", tt)], writes=[("outh", ch, tt)], sem="d_out")

        pr.op("sync", lambda h: h.nop(),
              extra=[pr.buf[("outh", ch, tt)][0] for ch in range(NCH) for tt in range(NTT)])

        pr.emit(block, sems)
    return nc


def _host_layouts(w_in, w_out, gup_packed, gup_scales, dn_packed, dn_scales):
    wT = np.ascontiguousarray(w_in.T)
    win_blk = np.ascontiguousarray(
        wT.reshape(NDT, P, 5, NH, P).transpose(3, 0, 1, 2, 4).reshape(NH, NDT, P, 5 * P))
    wout_blk = np.ascontiguousarray(w_out.T.reshape(NDT, P, D))
    gup_pk = np.ascontiguousarray(gup_packed.T)
    gup_sc = np.ascontiguousarray(np.repeat(gup_scales, 16, axis=0))
    dn_pk = np.ascontiguousarray(dn_packed.T)
    dn_sc = np.ascontiguousarray(np.repeat(dn_scales, 16, axis=0))
    idm = np.eye(P, dtype=np.float32).astype(ml_dtypes.bfloat16)
    cst = np.broadcast_to(np.array(CONST_VALS + [0.0, 0.0], np.float32), (P, 12)).copy()
    return win_blk, wout_blk, gup_pk, gup_sc, dn_pk, dn_sc, idm, cst


def kernel(L0_attn_x, L0_attn_w_in, L0_attn_w_out, L0_ffn_gate_up_packed,
           L0_ffn_gate_up_scales, L0_ffn_down_packed, L0_ffn_down_scales):
    x = np.ascontiguousarray(np.asarray(L0_attn_x, dtype=np.float32))
    B, S, Dm = x.shape
    assert Dm == D and B == 8
    win_blk, wout_blk, gup_pk, gup_sc, dn_pk, dn_sc, idm, cst = _host_layouts(
        np.asarray(L0_attn_w_in, np.float32), np.asarray(L0_attn_w_out, np.float32),
        np.asarray(L0_ffn_gate_up_packed, np.int32),
        np.asarray(L0_ffn_gate_up_scales, np.float16),
        np.asarray(L0_ffn_down_packed, np.int32),
        np.asarray(L0_ffn_down_scales, np.float16))
    import os as _os
    nc = build_nc(S, stage=_os.environ.get("KN_STAGE", "full"))
    shared = dict(win_blk=win_blk, wout_blk=wout_blk, gup_pk=gup_pk, gup_sc=gup_sc,
                  dn_pk=dn_pk, dn_sc=dn_sc, idm=idm, cst=cst)
    in_maps = [dict(x_in=np.ascontiguousarray(x[c]), **shared) for c in range(B)]
    res = run_bass_kernel_spmd(nc, in_maps, core_ids=list(range(B)))
    out = np.stack([np.asarray(res.results[c]["out"], dtype=np.float32) for c in range(B)])
    return out
